# version 16
# baseline (speedup 1.0000x reference)
import numpy as np
from contextlib import ExitStack

import concourse.bass as bass
import concourse.mybir as mybir
from concourse.bass_utils import run_bass_kernel_spmd

F32 = mybir.dt.float32
BF16 = mybir.dt.bfloat16
AF = mybir.ActivationFunctionType
ALU = mybir.AluOpType

LN_EPS = 1e-5
LN2 = 0.6931471805599453
TINY = 1.1754943508222875e-38

RACE_SAFE = False
import os
SKIP_SMALL_MM = os.environ.get('SKIP_SMALL_MM', '0') == '1'
BF16_STATS_TEST = os.environ.get('BF16_STATS_TEST', '0') == '1'


class _Op:
    __slots__ = ("stream", "fn", "inc_amt", "needs_inc", "seq", "val", "waits",
                 "tag", "gidx")

    def __init__(self, stream, fn, inc_amt, needs_inc, seq, tag):
        self.stream = stream
        self.fn = fn
        self.inc_amt = inc_amt
        self.needs_inc = needs_inc
        self.seq = seq
        self.val = None
        self.waits = []
        self.tag = tag


class Res:

    __slots__ = ("w", "readers", "name")

    def __init__(self, name=""):
        self.w = None
        self.readers = {}
        self.name = name


class Sched:
    NQS = 8
    NQA = 4
    NQG = 2
    STREAMS = (("pe", "dve", "act", "sync", "gp")
               + tuple(f"qsync{i}" for i in range(NQS))
               + tuple(f"qact{i}" for i in range(NQA))
               + tuple(f"qgp{i}" for i in range(NQG)))
    ISSUER = {}
    for i in range(NQS):
        ISSUER[f"qsync{i}"] = "sync"
    for i in range(NQA):
        ISSUER[f"qact{i}"] = "act"
    for i in range(NQG):
        ISSUER[f"qgp{i}"] = "gp"

    def __init__(self):
        self.ops = {s: [] for s in self.STREAMS}
        self.marks = {s: {} for s in self.STREAMS}
        self.qrr = {"qsync": 0, "qact": 0, "qgp": 0}

    race_safe = False
    skip_same = ("pe",)
    skip_dma_issue = ()

    def _dep(self, op, prod):
        if prod is None:
            return
        ps = prod.stream
        cs = op.stream
        if not self.race_safe:
            p_dma = ps in self.ISSUER
            c_dma = cs in self.ISSUER
            if not p_dma:
                if not c_dma and ps == cs and ps in self.skip_same:
                    return
                if c_dma and self.ISSUER[cs] == ps and ps in self.skip_dma_issue:
                    return
        m = self.marks[cs].get(ps, -1)
        if prod.seq <= m:
            return
        prod.needs_inc = True
        op.waits.append(prod)
        self.marks[cs][ps] = prod.seq

    def add(self, stream, fn, reads=(), writes=(), tag=""):
        dma = stream in ("qsync", "qact", "qgp")
        prev_slot_op = None
        if dma:
            n = {"qsync": self.NQS, "qact": self.NQA, "qgp": self.NQG}[stream]
            slot = self.qrr[stream] % n
            self.qrr[stream] += 1
            stream = f"{stream}{slot}"
            if self.ops[stream]:
                prev_slot_op = self.ops[stream][-1]
        op = _Op(stream, fn, 16 if dma else 1, dma, len(self.ops[stream]), tag)
        if prev_slot_op is not None:
            self._dep(op, prev_slot_op)
        for r in reads:
            self._dep(op, r.w)
        for r in writes:
            self._dep(op, r.w)
            for rd in r.readers.values():
                self._dep(op, rd)
        self.ops[stream].append(op)
        for r in reads:
            prev = r.readers.get(stream)
            if prev is None or prev.seq < op.seq:
                r.readers[stream] = op
        for r in writes:
            r.w = op
            r.readers = {}
        return op

    def finalize(self):
        for s in self.STREAMS:
            c = 0
            for op in self.ops[s]:
                if op.needs_inc:
                    c += op.inc_amt
                    op.val = c

    def replay(self, stream, eng, sems):
        issued = self.ops[stream]
        for op in issued:
            for prod in op.waits:
                assert prod.val is not None, (stream, op.tag, prod.stream, prod.tag)
                eng.wait_ge(sems[prod.stream], prod.val)
            instr = op.fn(eng)
            if op.needs_inc:
                instr.then_inc(sems[op.stream], op.inc_amt)

    def replay_engine(self, eng_name, eng, sems):
        streams = [eng_name]
        for q, issuer in self.ISSUER.items():
            if issuer == eng_name:
                streams.append(q)
        if len(streams) == 1:
            self.replay(eng_name, eng, sems)
            return
        merged = []
        for s in streams:
            merged.extend((op.gidx, s, op) for op in self.ops[s])
        merged.sort(key=lambda t: t[0])
        for _, s, op in merged:
            for prod in op.waits:
                assert prod.val is not None
                eng.wait_ge(sems[prod.stream], prod.val)
            instr = op.fn(eng)
            if op.needs_inc:
                instr.then_inc(sems[op.stream], op.inc_amt)


_GIDX = [0]


def _stamp(op):
    op.gidx = _GIDX[0]
    _GIDX[0] += 1
    return op


def build_nc(D=2048, HF=4096, TC=2048, TT=512):
    _GIDX[0] = 0
    DK = D // 128
    HK = HF // 128
    NH = DK
    NTILES = TC // TT
    H2SEG = HK // 2
    inv_D = 1.0 / D
    assert H2SEG * 2 == HK

    nc = bass.Bass("TRN2", target_bir_lowering=False,
                   detect_race_conditions=RACE_SAFE)

    xin = nc.dram_tensor("xin", [D, TC], F32, kind="ExternalInput").ap()
    out = nc.dram_tensor("out", [D, TC], F32, kind="ExternalOutput").ap()

    wsrc = {}
    wdims = {"w1": (DK, HK), "w2": (HK, HK), "w3": (HK, DK),
             "wq": (DK, DK), "wv": (DK, DK), "wg": (DK, DK), "wo": (DK, DK)}
    for name, (nk, no) in wdims.items():
        wsrc[name] = nc.dram_tensor(name, [no, 128, nk * 128], F32,
                                    kind="ExternalInput").ap()
    whi, wlo = {}, {}
    for name, (nk, no) in wdims.items():
        whi[name] = nc.dram_tensor(name + "_hi", [no, 128, nk * 128], BF16,
                                   kind="Internal").ap()
        wlo[name] = nc.dram_tensor(name + "_lo", [no, 128, nk * 128], BF16,
                                   kind="Internal").ap()

    x1sp = nc.dram_tensor("x1sp", [D, TC], F32, kind="Internal").ap()
    cc_in = nc.dram_tensor("cc_in", [128, NH], F32, kind="Internal").ap()
    cc_out = nc.dram_tensor("cc_out", [8 * 128, NH], F32, kind="Internal").ap()
    erowin = nc.dram_tensor("erow", [1, 8], F32, kind="ExternalInput").ap()

    b1in = nc.dram_tensor("b1c", [128, HK], F32, kind="ExternalInput").ap()
    b2in = nc.dram_tensor("b2c", [128, HK], F32, kind="ExternalInput").ap()
    b3in = nc.dram_tensor("b3c", [128, DK], F32, kind="ExternalInput").ap()
    gammain = nc.dram_tensor("gammac", [128, DK], F32, kind="ExternalInput").ap()
    betain = nc.dram_tensor("betac", [128, DK], F32, kind="ExternalInput").ap()
    hbetain = nc.dram_tensor("hbeta", [1, NH], F32, kind="ExternalInput").ap()

    es = ExitStack()
    sbuf = lambda n, s, d: es.enter_context(nc.sbuf_tensor(n, s, d)).ap()
    psum = lambda n, s: es.enter_context(nc.psum_tensor(n, s, F32)).ap()

    NROT = 6
    UA = DK * TT
    UB = HK * TT
    UC = NROT * TT
    U = sbuf("U", [128, UA + UB + UC], F32)

    def uview(off_f32, n_f32, dtype=F32):
        v = U[:, off_f32:off_f32 + n_f32]
        if dtype == BF16:
            v = v.bitcast(BF16)
        return v

    xh_v = uview(0, DK * TT // 2, BF16)
    xl_v = uview(DK * TT // 2, DK * TT // 2, BF16)
    h2h_v = uview(0, H2SEG * TT // 2, BF16)
    h2l_v = uview(H2SEG * TT // 2, H2SEG * TT // 2, BF16)
    h1h_v = uview(UA, HK * TT // 2, BF16)
    h1l_v = uview(UA + HK * TT // 2, HK * TT // 2, BF16)
    o1_v = uview(UA, DK * TT, F32)
    zh_v = uview(UA + DK * TT, DK * TT // 2, BF16)
    zl_v = uview(UA + DK * TT + DK * TT // 2, DK * TT // 2, BF16)
    SPLIT_PIECE = min(16, UB // (4 * 128)) * 128
    sin_v = [uview(UA + i * SPLIT_PIECE, SPLIT_PIECE, F32) for i in range(2)]
    shi_v = [uview(UA + 2 * SPLIT_PIECE + i * SPLIT_PIECE // 2, SPLIT_PIECE // 2, BF16)
             for i in range(2)]
    slo_v = [uview(UA + 3 * SPLIT_PIECE + i * SPLIT_PIECE // 2, SPLIT_PIECE // 2, BF16)
             for i in range(2)]
    rot_v = [uview(UA + UB + i * TT, TT, F32) for i in range(NROT)]

    scb = sbuf("scb", [128, TT], F32)
    S0 = sbuf("S0", [128, DK * TT], F32)
    S1 = sbuf("S1", [128, DK * TT], F32)
    wbuf_h = [sbuf(f"wh{i}", [128, 16 * 128], BF16) for i in range(2)]
    wbuf_l = [sbuf(f"wl{i}", [128, 16 * 128], BF16) for i in range(2)]
    ones_col = sbuf("ones_col", [128, 1], F32)
    ones_row = sbuf("ones_row", [1, 128], F32)
    ones_mat = sbuf("ones_mat", [128, 128], F32)
    ones_mat_b = sbuf("ones_mat_b", [128, 128], BF16)
    mu_sb = sbuf("mu_sb", [128, 512 if 0 else 1] , F32) if False else sbuf("mu_sbx", [128, TT], F32)
    rb_sb = sbuf("rb_sbx", [128, TT], F32)
    lam_cols = sbuf("lam_cols", [128, NH], F32)
    state_cols = sbuf("state_cols", [128, NH], F32)
    b1c = sbuf("b1cs", [128, HK], F32)
    b2c = sbuf("b2cs", [128, HK], F32)
    b3c = sbuf("b3cs", [128, DK], F32)
    gac = sbuf("gacs", [128, DK], F32)
    bec = sbuf("becs", [128, DK], F32)
    hbeta = sbuf("hbetas", [1, NH], F32)
    r_lam = sbuf("r_lam", [1, NH], F32)
    epsc = sbuf("epsc", [128, 1], F32)
    erow = sbuf("erows", [1, 8], F32)
    r_lnlam = sbuf("r_lnlam", [1, NH], F32)
    r_pw = sbuf("r_pw", [1, NH], F32)
    g_sb = sbuf("g_sb", [128, 8 * NH], F32)
    st_tmp = sbuf("st_tmp", [128, NH], F32)

    pmm = [psum(f"pmm{i}", [128, TT]) for i in range(6)]
    pbc0 = psum("pbc0", [128, TT])
    pbc1 = psum("pbc1", [128, TT])

    sch = Sched()
    sch.race_safe = RACE_SAFE
    R = lambda n: Res(n)
    r_xin = R("xin")
    r_out = R("out")
    r_wsrc = {k: R("wsrc_" + k) for k in wdims}
    r_whi = {k: R("whi_" + k) for k in wdims}
    r_wlo = {k: R("wlo_" + k) for k in wdims}
    GRAN = TT // 2
    gA = [R(f"gA{i}") for i in range(UA // GRAN)]
    gB = [R(f"gB{i}") for i in range(UB // GRAN)]
    gS0 = [R(f"gS0_{i}") for i in range(DK)]
    gS1 = [R(f"gS1_{i}") for i in range(DK)]

    def gb_range(off_f32, n_f32):
        return gB[off_f32 // GRAN: -(-(off_f32 + n_f32) // GRAN)]

    r_rot = [R(f"rot{i}") for i in range(NROT)]
    r_wbh = [R(f"wbh{i}") for i in range(2)]
    r_wbl = [R(f"wbl{i}") for i in range(2)]
    r_consts = R("consts")
    r_state = R("state_cols")
    r_rows = {id(x): R("row") for x in ()}
    r_rmu, r_rm2, r_rt, r_rvar, r_rsd, r_rrr = (R("rmu"), R("rm2"), R("rt"),
                                                R("rvar"), R("rsd"), R("rrr"))
    r_pmm = [R(f"pmm{i}") for i in range(6)]
    r_pbc0, r_pbc1 = R("pbc0"), R("pbc1")
    r_lnl, r_erow, r_pw2 = R("lnlam"), R("erow"), R("pw")
    r_mub, r_rbb = R("mu_b"), R("r_b")
    r_scb = R("scb")
    r_x1sp, r_ccin, r_ccout, r_gsb, r_sttmp = (R("x1sp"), R("ccin"), R("ccout"),
                                               R("gsb"), R("sttmp"))

    def op(stream, fn, reads=(), writes=(), tag=""):
        return _stamp(sch.add(stream, fn, reads, writes, tag))

    op("dve", lambda e: e.memset(ones_col[:, :], 1.0), writes=[r_consts], tag="ones_col")
    op("dve", lambda e: e.memset(ones_row[:, :], 1.0), reads=[r_consts], writes=[r_consts], tag="ones_row")
    op("dve", lambda e: e.memset(ones_mat[:, :], 1.0), reads=[r_consts], writes=[r_consts], tag="ones_mat")
    op("dve", lambda e: e.memset(ones_mat_b[:, :], 1.0), reads=[r_consts], writes=[r_consts], tag="ones_mat_b")
    op("dve", lambda e: e.memset(state_cols[:, :], 0.0), writes=[r_state], tag="state0")
    op("dve", lambda e: e.memset(epsc[:, :], LN_EPS), reads=[r_consts],
       writes=[r_consts], tag="eps")
    for dst, src in ((b1c, b1in), (b2c, b2in), (gac, gammain), (bec, betain),
                     (hbeta, hbetain)):
        op("qsync", lambda e, d=dst, s=src: e.dma_start(out=d[:, :], in_=s[:, :]),
           writes=[r_consts], tag="ld_const")
    op("qsync", lambda e: e.dma_start(out=b3c[:, :], in_=b3in[:, :]),
       writes=[r_consts], tag="ld_b3")
    op("dve", lambda e: e.tensor_scalar(b3c[:, :], b3c[:, :], 0.5, None, ALU.mult),
       reads=[r_consts], writes=[r_consts], tag="b3half")
    op("act", lambda e: e.activation(r_lam[:, :], hbeta[:, :], AF.Exp, scale=-LN2),
       reads=[r_consts], writes=[r_rmu], tag="lam_exp")
    op("dve", lambda e: e.tensor_scalar(r_lam[:, :], r_lam[:, :], -1.0, 1.0,
                                        ALU.mult, ALU.add),
       reads=[r_rmu], writes=[r_rmu], tag="lam_1m")
    op("dve", lambda e: e.tensor_scalar(r_lam[:, :], r_lam[:, :], TINY, 1.0 - 1e-9,
                                        ALU.max, ALU.min),
       reads=[r_rmu], writes=[r_rmu], tag="lam_clip")
    op("pe", lambda e: e.matmul(pbc0[:, 0:NH], ones_row[:, :], r_lam[:, :],
                                start=True, stop=True),
       reads=[r_rmu, r_consts], writes=[r_pbc0], tag="lam_bcast")
    op("act", lambda e: e.copy(lam_cols[:, :], pbc0[:, 0:NH]),
       reads=[r_pbc0], writes=[r_consts], tag="lam_cols")
    op("act", lambda e: e.activation(r_lnlam[:, :], r_lam[:, :], AF.Ln),
       reads=[r_rmu], writes=[r_lnl], tag="lnlam")
    op("qsync", lambda e: e.dma_start(out=erow[:, :], in_=erowin[:, :]),
       writes=[r_erow], tag="ld_erow")

    scr_piece = {}
    r_wsc = {}

    def _mk_scr(name, piece):
        nk, no = wdims[name]
        scr_piece[name] = piece
        npieces = (nk * 128 + piece - 1) // piece
        r_wsc[name] = {pol: [[R(f"sc_{name}_{pol}_{oc}_{p}") for p in range(npieces)]
                             for oc in range(no)] for pol in ("hi", "lo")}

    def split_gen(name, bulk):
        nk, no = wdims[name]
        piece = scr_piece[name]
        cnt = [0]
        for oc in range(no):
            line = nk * 128
            for pidx, k0 in enumerate(range(0, line, piece)):
                n = min(piece, line - k0)
                if bulk:
                    rb = cnt[0] % 2
                    vin = sin_v[rb]
                    rin = gb_range(rb * SPLIT_PIECE, SPLIT_PIECE)
                    vhi, vlo = shi_v[rb], slo_v[rb]
                    rhl = gb_range(2 * SPLIT_PIECE + rb * SPLIT_PIECE // 2,
                                   SPLIT_PIECE // 2)
                    rhl2 = gb_range(3 * SPLIT_PIECE + rb * SPLIT_PIECE // 2,
                                    SPLIT_PIECE // 2)
                else:
                    vin, rin0 = rot()
                    slot, rhl0 = rot()
                    vhi = slot[:, 0:piece // 2].bitcast(BF16)
                    vlo = slot[:, piece // 2:piece].bitcast(BF16)
                    rin, rhl, rhl2 = [rin0], [rhl0], [rhl0]
                cnt[0] += 1
                op("qsync",
                   lambda e, oc=oc, k0=k0, n=n, vin=vin:
                       e.dma_start(out=vin[:, 0:n], in_=wsrc[name][oc, :, k0:k0 + n]),
                   reads=[r_wsrc[name]], writes=rin, tag=f"spl_ld_{name}")
                op("dve",
                   lambda e, n=n, vin=vin, vhi=vhi:
                       e.tensor_copy(vhi[:, 0:n], vin[:, 0:n]),
                   reads=rin, writes=rhl, tag="spl_hi")
                op("dve",
                   lambda e, n=n, vin=vin, vhi=vhi, vlo=vlo:
                       e.tensor_tensor(vlo[:, 0:n], vin[:, 0:n], vhi[:, 0:n],
                                       ALU.subtract),
                   reads=rin + rhl, writes=rhl2, tag="spl_lo")
                op("qact",
                   lambda e, oc=oc, k0=k0, n=n, vhi=vhi:
                       e.dma_start(out=whi[name][oc, :, k0:k0 + n], in_=vhi[:, 0:n]),
                   reads=rhl, writes=[r_wsc[name]["hi"][oc][pidx]],
                   tag=f"spl_st_hi_{name}")
                op("qact",
                   lambda e, oc=oc, k0=k0, n=n, vlo=vlo:
                       e.dma_start(out=wlo[name][oc, :, k0:k0 + n], in_=vlo[:, 0:n]),
                   reads=rhl2, writes=[r_wsc[name]["lo"][oc][pidx]],
                   tag=f"spl_st_lo_{name}")
                produced[name] += 1
                yield

    produced = {name: 0 for name in wdims}
    pump_q = []

    def pump(n):
        while n > 0 and pump_q:
            try:
                next(pump_q[0])
                n -= 1
            except StopIteration:
                pump_q.pop(0)

    def pump_for(name, npieces):
        while produced[name] < npieces and pump_q:
            try:
                next(pump_q[0])
            except StopIteration:
                pump_q.pop(0)

    BULK = ("w1", "w2")
    PUMPED = ("w3", "wv", "wq", "wg", "wo")
    for name in BULK:
        _mk_scr(name, SPLIT_PIECE)
    for name in PUMPED:
        _mk_scr(name, TT)
    for name in BULK:
        for _ in split_gen(name, bulk=True):
            pass
    for name in PUMPED:
        pump_q.append(split_gen(name, bulk=False))

    x3 = xin.rearrange("(k p) t -> p k t", p=128)
    out3 = out.rearrange("(k p) t -> p k t", p=128)

    def load_xtile(ti):
        t0 = ti * TT
        src = x3[:, :, t0:t0 + TT]
        dst = S0.rearrange("p (k t) -> p k t", t=TT)
        op("qsync", lambda e, src=src, dst=dst: e.dma_start(out=dst, in_=src),
           reads=[r_xin], writes=gS0, tag=f"x_ld{ti}")

    def store_tile(ti):
        t0 = ti * TT
        dst = out3[:, :, t0:t0 + TT]
        src = S1.rearrange("p (k t) -> p k t", t=TT)
        op("qact", lambda e, src=src, dst=dst: e.dma_start(out=dst, in_=src),
           reads=gS1, writes=[r_out], tag=f"out_st{ti}")

    rotc = [0]

    def rot():
        i = rotc[0] % NROT
        rotc[0] += 1
        return rot_v[i], r_rot[i]

    sc_state = {"n": 0}

    def stats_pair_ops(sl_ap, rS, first, last, tag, sqv, sq_res):
        op("act", lambda e, sl=sl_ap, rq=sqv: e.activation(rq[:, :], sl, AF.Square),
           reads=rS, writes=sq_res, tag=tag + "_sq")
        op("pe", lambda e, sl=sl_ap, first=first, last=last: e.matmul(
            pbc0[:, :], ones_mat[:, :], sl, start=first, stop=last),
           reads=rS + [r_consts], writes=[r_pbc0], tag=tag + "_s1")
        op("pe", lambda e, rq=sqv, first=first, last=last: e.matmul(
            pbc1[:, :], ones_mat[:, :], rq[:, :], start=first, stop=last),
           reads=sq_res + [r_consts], writes=[r_pbc1], tag=tag + "_s2")

    def stats_norm(S, rS, res_h, res_l, xh, xl, tag, stats_done=False):
        if not stats_done:
            for kc in range(DK):
                sl = S[:, kc * TT:(kc + 1) * TT]
                qv, qr = sqA(kc)
                stats_pair_ops(sl, [rS[kc]], kc == 0, kc == DK - 1,
                               tag + f"_st{kc}", qv, qr)
        finish_stats(tag)
        for kc in range(DK):
            sl = S[:, kc * TT:(kc + 1) * TT]
            rv0, rr0 = rot()
            rv1, rr1 = rot()
            op("dve", lambda e, sl=sl, rv0=rv0: e.tensor_tensor(
                rv0[:, :], sl, mu_sb[:, :], ALU.subtract),
               reads=[rS[kc], r_mub], writes=[rr0], tag=tag + "_sub")
            op("dve", lambda e, rv0=rv0, rv1=rv1: e.tensor_tensor(
                rv1[:, :], rv0[:, :], rb_sb[:, :], ALU.mult),
               reads=[rr0, r_rbb], writes=[rr1], tag=tag + "_mul")
            hsl = xh[:, kc * TT:(kc + 1) * TT]
            lsl = xl[:, kc * TT:(kc + 1) * TT]
            op("act", lambda e, rv1=rv1, hsl=hsl: e.copy(hsl, rv1[:, :]),
               reads=[rr1], writes=[res_h[kc]], tag=tag + "_xh")
            op("dve", lambda e, rv1=rv1, hsl=hsl, lsl=lsl: e.tensor_tensor(
                lsl, rv1[:, :], hsl, ALU.subtract),
               reads=[rr1, res_h[kc]], writes=[res_l[kc]], tag=tag + "_xl")

    def finish_stats(tag):
        rv0, rr0 = rot()
        rv1, rr1 = rot()
        op("dve", lambda e: e.tensor_scalar(mu_sb[:, :], pbc0[:, :], inv_D, None,
                                            ALU.mult),
           reads=[r_pbc0], writes=[r_mub], tag=tag + "_mu")
        op("dve", lambda e, rv0=rv0: e.tensor_scalar(rv0[:, :], pbc1[:, :], inv_D,
                                                     None, ALU.mult),
           reads=[r_pbc1], writes=[rr0], tag=tag + "_m2")
        op("dve", lambda e, rv1=rv1: e.tensor_tensor(
            rv1[:, :], mu_sb[:, :], mu_sb[:, :], ALU.mult),
           reads=[r_mub], writes=[rr1], tag=tag + "_musq")
        op("dve", lambda e, rv0=rv0, rv1=rv1: e.tensor_tensor(
            rv0[:, :], rv0[:, :], rv1[:, :], ALU.subtract),
           reads=[rr0, rr1], writes=[rr0], tag=tag + "_var")
        op("act", lambda e, rv0=rv0: e.activation(rv0[:, :], rv0[:, :], AF.Sqrt,
                                                  bias=epsc[:, :]),
           reads=[rr0, r_consts], writes=[rr0], tag=tag + "_sd")
        op("dve", lambda e, rv0=rv0: e.reciprocal(rb_sb[:, :], rv0[:, :]),
           reads=[rr0], writes=[r_rbb], tag=tag + "_rr")

    wrot = [0]

    def mm_group(wname, oc, nkc, kc_base, mvh, mvl, res_h, res_l, ps_i, tag):
        nk, no = wdims[wname]
        ps = pmm[ps_i]
        rps = r_pmm[ps_i]
        pieces = [(p0, min(nkc, p0 + 16)) for p0 in range(0, nkc, 16)]
        first = True
        if wname in PUMPED:
            pe_sz = scr_piece[wname]
            need = -(-((kc_base + nkc) * 128) // pe_sz)
            pump_for(wname, oc * ((wdims[wname][0] * 128 + pe_sz - 1) // pe_sz) + need)
        pump(3)
        for (p0, p1) in pieces:
            wb = wrot[0] % 2
            wrot[0] += 1
            npc = (p1 - p0) * 128
            k0 = (kc_base + p0) * 128
            pe_sz = scr_piece[wname]
            i0, i1 = k0 // pe_sz, -(-(k0 + npc) // pe_sz)
            sc_hi = r_wsc[wname]["hi"][oc][i0:i1]
            sc_lo = r_wsc[wname]["lo"][oc][i0:i1]
            for rr_ in sc_hi + sc_lo:
                assert rr_.w is not None, (wname, oc, tag, "scratch not yet split")
            op("qsync", lambda e, oc=oc, k0=k0, npc=npc, wb=wb: e.dma_start(
                out=wbuf_h[wb][:, 0:npc], in_=whi[wname][oc, :, k0:k0 + npc]),
               reads=sc_hi, writes=[r_wbh[wb]], tag=tag + "_lh")
            op("qsync", lambda e, oc=oc, k0=k0, npc=npc, wb=wb: e.dma_start(
                out=wbuf_l[wb][:, 0:npc], in_=wlo[wname][oc, :, k0:k0 + npc]),
               reads=sc_lo, writes=[r_wbl[wb]], tag=tag + "_ll")
            last_piece = (p1 == nkc)
            for kc in range(p0, p1):
                koff = (kc - p0) * 128
                mh = mvh[:, kc * TT:(kc + 1) * TT]
                ml = mvl[:, kc * TT:(kc + 1) * TT]
                last = last_piece and (kc == p1 - 1)
                op("pe", lambda e, ps=ps, wb=wb, koff=koff, mh=mh, first=first:
                   e.matmul(ps[:, :], wbuf_h[wb][:, koff:koff + 128], mh,
                            start=first, stop=False),
                   reads=[r_wbh[wb], res_h(kc)], writes=[rps], tag=tag + "_hh")
                first = False
                op("pe", lambda e, ps=ps, wb=wb, koff=koff, ml=ml:
                   e.matmul(ps[:, :], wbuf_h[wb][:, koff:koff + 128], ml,
                            start=False, stop=False),
                   reads=[r_wbh[wb], res_l(kc)], writes=[rps], tag=tag + "_hl")
                op("pe", lambda e, ps=ps, wb=wb, koff=koff, mh=mh, last=last:
                   e.matmul(ps[:, :], wbuf_l[wb][:, koff:koff + 128], mh,
                            start=False, stop=last),
                   reads=[r_wbl[wb], res_h(kc)], writes=[rps], tag=tag + "_lh")
        return ps, rps

    WLAG = 4

    xg_h = lambda kc: gA[kc]
    xg_l = lambda kc: gA[DK + kc]
    h1g_h = lambda kc: gB[kc]
    h1g_l = lambda kc: gB[HK + kc]
    h2g_h = lambda kc: gA[kc]
    h2g_l = lambda kc: gA[H2SEG + kc]
    zg_h = lambda kc: gB[2 * DK + kc]
    zg_l = lambda kc: gB[3 * DK + kc]
    o1g = lambda hc: [gB[2 * hc], gB[2 * hc + 1]]

    sqA = lambda kc: (uview(kc * TT, TT), [gA[2 * kc], gA[2 * kc + 1]])
    sqB = lambda c: (uview(UA + c * TT, TT), [gB[2 * c], gB[2 * c + 1]])
    sqZ = lambda c: (uview(UA + DK * TT + c * TT, TT),
                     [gB[2 * DK + 2 * c], gB[2 * DK + 2 * c + 1]])

    def ffn(rSin, Sin, rSout, Sout, tag, stats_done=False, weave_out=False):
        pend = []
        stats_norm(Sin, rSin, [xg_h(k) for k in range(DK)],
                   [xg_l(k) for k in range(DK)], xh_v, xl_v, tag + "_ln",
                   stats_done=stats_done)
        for oc in range(HK):
            ps, rps = mm_group("w1", oc, DK, 0, xh_v, xl_v, xg_h, xg_l, oc % 6, tag + f"_w1_{oc}")
            rv, rr = rot()
            op("act", lambda e, ps=ps, rv=rv, oc=oc: e.activation(
                rv[:, :], ps[:, :], AF.Relu, bias=b1c[:, oc:oc + 1]),
               reads=[rps, r_consts], writes=[rr], tag=tag + "_h1ev")
            hsl = h1h_v[:, oc * TT:(oc + 1) * TT]
            lsl = h1l_v[:, oc * TT:(oc + 1) * TT]
            op("act", lambda e, rv=rv, hsl=hsl: e.copy(hsl, rv[:, :]),
               reads=[rr], writes=[h1g_h(oc)], tag=tag + "_h1h")
            op("dve", lambda e, rv=rv, hsl=hsl, lsl=lsl: e.tensor_tensor(
                lsl, rv[:, :], hsl, ALU.subtract),
               reads=[rr, h1g_h(oc)], writes=[h1g_l(oc)], tag=tag + "_h1l")
        for seg in range(2):
            for i in range(H2SEG):
                oc2 = seg * H2SEG + i
                ps, rps = mm_group("w2", oc2, HK, 0, h1h_v, h1l_v, h1g_h, h1g_l,
                                   oc2 % 6, tag + f"_w2_{oc2}")
                rv, rr = rot()
                op("act", lambda e, ps=ps, rv=rv, oc2=oc2: e.activation(
                    rv[:, :], ps[:, :], AF.Relu, bias=b2c[:, oc2:oc2 + 1]),
                   reads=[rps, r_consts], writes=[rr], tag=tag + "_h2ev")
                hsl = h2h_v[:, i * TT:(i + 1) * TT]
                lsl = h2l_v[:, i * TT:(i + 1) * TT]
                op("act", lambda e, rv=rv, hsl=hsl: e.copy(hsl, rv[:, :]),
                   reads=[rr], writes=[h2g_h(i)], tag=tag + "_h2h")
                op("dve", lambda e, rv=rv, hsl=hsl, lsl=lsl: e.tensor_tensor(
                    lsl, rv[:, :], hsl, ALU.subtract),
                   reads=[rr, h2g_h(i)], writes=[h2g_l(i)], tag=tag + "_h2l")
            for oc3 in range(DK):
                ps, rps = mm_group("w3", oc3, H2SEG, seg * H2SEG, h2h_v, h2l_v,
                                   h2g_h, h2g_l, oc3 % 6, tag + f"_w3_{seg}_{oc3}")
                ssl_in = Sin[:, oc3 * TT:(oc3 + 1) * TT]
                ssl_out = Sout[:, oc3 * TT:(oc3 + 1) * TT]
                if seg == 0:
                    rv, rr = rot()
                    op("act", lambda e, ps=ps, rv=rv, oc3=oc3: e.activation(
                        rv[:, :], ps[:, :], AF.Identity, bias=b3c[:, oc3:oc3 + 1],
                        scale=0.5),
                       reads=[rps, r_consts], writes=[rr], tag=tag + "_w3ev")
                    op("dve", lambda e, rv=rv, a=ssl_in, o=ssl_out: e.tensor_tensor(
                        o, rv[:, :], a, ALU.add),
                       reads=[rr, rSin[oc3]], writes=[rSout[oc3]], tag=tag + "_res")
                else:
                    op("dve", lambda e, ps=ps, o=ssl_out: e.scalar_tensor_tensor(
                        o, ps[:, :], 0.5, o, ALU.mult, ALU.add),
                       reads=[rps, rSout[oc3]], writes=[rSout[oc3]], tag=tag + "_res2")
                    if weave_out:
                        pend.append((ssl_out, oc3))
                        if len(pend) > WLAG:
                            a_, i_ = pend.pop(0)
                            qv, qr = sqB(i_)
                            stats_pair_ops(a_, [rSout[i_]], i_ == 0, i_ == DK - 1,
                                           tag + f"_wst{i_}", qv, qr)

        for a_, i_ in pend:
            qv, qr = sqB(i_)
            stats_pair_ops(a_, [rSout[i_]], i_ == 0, i_ == DK - 1, tag + f"_wst{i_}",
                           qv, qr)

    def spill_x1(ti):
        t0 = ti * TT
        x1sp3 = x1sp.rearrange("(k p) t -> p k t", p=128)
        dst = x1sp3[:, :, t0:t0 + TT]
        src = S1.rearrange("p (k t) -> p k t", t=TT)
        op("qact", lambda e, src=src, dst=dst: e.dma_start(out=dst, in_=src),
           reads=gS1, writes=[r_x1sp], tag=f"x1_sp{ti}")

    def load_x1(ti):
        t0 = ti * TT
        x1sp3 = x1sp.rearrange("(k p) t -> p k t", p=128)
        src = x1sp3[:, :, t0:t0 + TT]
        dst = S1.rearrange("p (k t) -> p k t", t=TT)
        op("qsync", lambda e, src=src, dst=dst: e.dma_start(out=dst, in_=src),
           reads=[r_x1sp], writes=gS1, tag=f"x1_ld{ti}")

    def attn_front(rSin, Sin, tag, stats_done=False):
        stats_norm(Sin, rSin, [xg_h(k) for k in range(DK)],
                   [xg_l(k) for k in range(DK)], xh_v, xl_v, tag + "_ln",
                   stats_done=stats_done)
        for hc in range(NH):
            ps, rps = mm_group("wv", hc, DK, 0, xh_v, xl_v, xg_h, xg_l, hc % 3, tag + f"_v{hc}")
            rv, rrv = rot()
            op("dve", lambda e, ps=ps, rv=rv: e.tensor_copy(rv[:, :], ps[:, :]),
               reads=[rps], writes=[rrv], tag=tag + "_vev")
            rrt, rrrt = scb, r_scb
            lam_b = lam_cols[:, hc:hc + 1].broadcast_to([128, TT])
            op("dve", lambda e, rv=rv, rrt=rrt, lam_b=lam_b, hc=hc:
               e.tensor_tensor_scan(rrt[:, :], lam_b, rv[:, :],
                                    state_cols[:, hc:hc + 1], ALU.mult, ALU.add),
               reads=[rrv, r_consts, r_state], writes=[rrrt], tag=tag + "_scan")
            op("dve", lambda e, rrt=rrt, hc=hc: e.tensor_copy(
                state_cols[:, hc:hc + 1], rrt[:, TT - 1:TT]),
               reads=[rrrt], writes=[r_state], tag=tag + "_stupd")

    def exchange(tag="xc"):
        op("qgp", lambda e: e.dma_start(out=cc_in[:, :], in_=state_cols[:, :]),
           reads=[r_state], writes=[r_ccin], tag=tag + "_in")
        op("gp", lambda e: e.collective_compute(
            "AllGather", ALU.bypass, replica_groups=[list(range(8))],
            ins=[cc_in[:, :].opt()], outs=[cc_out[:, :].opt()]),
           reads=[r_ccin], writes=[r_ccout], tag=tag + "_ag")
        src = cc_out.rearrange("(i p) h -> p i h", p=128)
        dst = g_sb.rearrange("p (i h) -> p i h", h=NH)
        op("qgp", lambda e, src=src, dst=dst: e.dma_start(out=dst, in_=src),
           reads=[r_ccout], writes=[r_gsb], tag=tag + "_out")
        op("dve", lambda e: e.memset(state_cols[:, :], 0.0),
           reads=[r_state], writes=[r_state], tag=tag + "_zero")
        for i in range(8):
            op("dve", lambda e, i=i: e.tensor_scalar(
                r_pw[:, :], r_lnlam[:, :], erow[0:1, i:i + 1], None, ALU.mult),
               reads=[r_lnl, r_erow], writes=[r_pw2], tag=tag + "_pw")
            op("act", lambda e: e.activation(r_pw[:, :], r_pw[:, :], AF.Exp),
               reads=[r_pw2], writes=[r_pw2], tag=tag + "_pwe")
            op("pe", lambda e: e.matmul(pbc0[:, 0:NH], ones_row[:, :], r_pw[:, :],
                                        start=True, stop=True),
               reads=[r_pw2, r_consts], writes=[r_pbc0], tag=tag + "_pwb")
            op("dve", lambda e, i=i: e.tensor_tensor(
                st_tmp[:, :], g_sb[:, i * NH:(i + 1) * NH], pbc0[:, 0:NH], ALU.mult),
               reads=[r_gsb, r_pbc0], writes=[r_sttmp], tag=tag + "_mul")
            op("dve", lambda e: e.tensor_tensor(
                state_cols[:, :], state_cols[:, :], st_tmp[:, :], ALU.add),
               reads=[r_state, r_sttmp], writes=[r_state], tag=tag + "_acc")

    def attn(rSin, Sin, rSout, Sout, tag, stats_done=False, weave_out=False):
        pend2 = []
        stats_norm(Sin, rSin, [xg_h(k) for k in range(DK)],
                   [xg_l(k) for k in range(DK)], xh_v, xl_v, tag + "_ln",
                   stats_done=stats_done)
        for hc in range(NH):
            ps, rps = mm_group("wv", hc, DK, 0, xh_v, xl_v, xg_h, xg_l, hc % 3, tag + f"_v{hc}")
            rv, rrv = rot()
            op("dve", lambda e, ps=ps, rv=rv: e.tensor_copy(rv[:, :], ps[:, :]),
               reads=[rps], writes=[rrv], tag=tag + "_vev")
            rrt, rrrt = scb, r_scb
            lam_b = lam_cols[:, hc:hc + 1].broadcast_to([128, TT])
            op("dve", lambda e, rv=rv, rrt=rrt, lam_b=lam_b, hc=hc:
               e.tensor_tensor_scan(rrt[:, :], lam_b, rv[:, :],
                                    state_cols[:, hc:hc + 1], ALU.mult, ALU.add),
               reads=[rrv, r_consts, r_state], writes=[rrrt], tag=tag + "_scan")
            op("dve", lambda e, rrt=rrt, hc=hc: e.tensor_copy(
                state_cols[:, hc:hc + 1], rrt[:, TT - 1:TT]),
               reads=[rrrt], writes=[r_state], tag=tag + "_stupd")
            ps2, rps2 = mm_group("wq", hc, DK, 0, xh_v, xl_v, xg_h, xg_l, 3 + hc % 3,
                                 tag + f"_q{hc}")
            osl = o1_v[:, hc * TT:(hc + 1) * TT]
            op("dve", lambda e, ps2=ps2, rrt=rrt, osl=osl: e.tensor_tensor(
                osl, ps2[:, :], rrt[:, :], ALU.mult),
               reads=[rps2, rrrt], writes=o1g(hc), tag=tag + "_o1")
            qv, qr = sqZ(hc)
            stats_pair_ops(osl, o1g(hc), hc == 0, hc == NH - 1, tag + f"_ist{hc}",
                           qv, qr)
        finish_stats(tag + "_i")
        for hc in range(NH):
            ps, rps = mm_group("wg", hc, DK, 0, xh_v, xl_v, xg_h, xg_l, hc % 3, tag + f"_g{hc}")
            rsg, rrsg = rot()
            op("act", lambda e, ps=ps, rsg=rsg: e.activation(rsg[:, :], ps[:, :], AF.Silu),
               reads=[rps], writes=[rrsg], tag=tag + "_silu")
            osl = o1_v[:, hc * TT:(hc + 1) * TT]
            rv0, rr0 = rot()
            op("dve", lambda e, osl=osl, rv0=rv0: e.tensor_tensor(
                rv0[:, :], osl, mu_sb[:, :], ALU.subtract),
               reads=o1g(hc) + [r_mub], writes=[rr0], tag=tag + "_ynsub")
            op("dve", lambda e, rv0=rv0: e.tensor_tensor(
                rv0[:, :], rv0[:, :], rb_sb[:, :], ALU.mult),
               reads=[rr0, r_rbb], writes=[rr0], tag=tag + "_ynmul")
            op("dve", lambda e, rv0=rv0, hc=hc: e.tensor_scalar(
                rv0[:, :], rv0[:, :], gac[:, hc:hc + 1], bec[:, hc:hc + 1],
                ALU.mult, ALU.add),
               reads=[rr0, r_consts], writes=[rr0], tag=tag + "_gb")
            op("dve", lambda e, rv0=rv0, rsg=rsg: e.tensor_tensor(
                rv0[:, :], rv0[:, :], rsg[:, :], ALU.mult),
               reads=[rr0, rrsg], writes=[rr0], tag=tag + "_gate")
            hsl = zh_v[:, hc * TT:(hc + 1) * TT]
            lsl = zl_v[:, hc * TT:(hc + 1) * TT]
            op("act", lambda e, rv0=rv0, hsl=hsl: e.copy(hsl, rv0[:, :]),
               reads=[rr0], writes=[zg_h(hc)], tag=tag + "_zh")
            op("dve", lambda e, rv0=rv0, hsl=hsl, lsl=lsl: e.tensor_tensor(
                lsl, rv0[:, :], hsl, ALU.subtract),
               reads=[rr0, zg_h(hc)], writes=[zg_l(hc)], tag=tag + "_zl")
        for oc in range(DK):
            ps, rps = mm_group("wo", oc, DK, 0, zh_v, zl_v, zg_h, zg_l, oc % 6, tag + f"_wo{oc}")
            a = Sin[:, oc * TT:(oc + 1) * TT]
            o = Sout[:, oc * TT:(oc + 1) * TT]
            op("dve", lambda e, ps=ps, a=a, o=o: e.tensor_tensor(o, ps[:, :], a, ALU.add),
               reads=[rps, rSin[oc]], writes=[rSout[oc]], tag=tag + "_res")
            if weave_out:
                pend2.append((o, oc))
                if len(pend2) > WLAG:
                    a_, i_ = pend2.pop(0)
                    qv, qr = sqB(i_)
                    stats_pair_ops(a_, [rSout[i_]], i_ == 0, i_ == DK - 1,
                                   tag + f"_wst{i_}", qv, qr)
        for a_, i_ in pend2:
            qv, qr = sqB(i_)
            stats_pair_ops(a_, [rSout[i_]], i_ == 0, i_ == DK - 1, tag + f"_wst{i_}",
                           qv, qr)

    for ti in range(NTILES):
        load_xtile(ti)
        ffn(gS0, S0, gS1, S1, f"t{ti}f1", weave_out=True)
        spill_x1(ti)
        attn_front(gS1, S1, f"t{ti}af", stats_done=True)
    exchange()
    for ti in range(NTILES):
        load_x1(ti)
        attn(gS1, S1, gS0, S0, f"t{ti}at", weave_out=True)
        ffn(gS0, S0, gS1, S1, f"t{ti}f2", stats_done=True)
        store_tile(ti)

    sch.finalize()
    sems = {}
    for s in Sched.STREAMS:
        sems[s] = es.enter_context(nc.semaphore("sem_" + s))

    final_waits = []
    for s in Sched.STREAMS:
        if s.startswith("qact") and sch.ops[s]:
            last = sch.ops[s][-1]
            assert last.val is not None
            final_waits.append((s, last.val))

    with nc.Block() as block:
        @block.tensor
        def _(eng):
            sch.replay_engine("pe", eng, sems)

        @block.vector
        def _(eng):
            sch.replay_engine("dve", eng, sems)

        @block.scalar
        def _(eng):
            sch.replay_engine("act", eng, sems)

        @block.gpsimd
        def _(eng):
            sch.replay_engine("gp", eng, sems)

        @block.sync
        def _(eng):
            sch.replay_engine("sync", eng, sems)
            for s, v in final_waits:
                eng.wait_ge(sems[s], v)

    es.close()
    return nc


def _blockT(W):
    WT = np.ascontiguousarray(W.T)
    K, O = WT.shape
    nk, no = K // 128, O // 128
    a = WT.reshape(nk, 128, no, 128).transpose(2, 1, 0, 3)
    return np.ascontiguousarray(a.reshape(no, 128, nk * 128))


def _colmajor(v):
    return np.ascontiguousarray(v.reshape(-1, 128).T)


def kernel(L0_attn_x, L0_attn_Wq, L0_attn_Wv, L0_attn_Wo, L0_attn_Wg,
           L0_attn_ln_gamma, L0_attn_ln_beta, L0_attn_beta,
           L0_ffn_W1, L0_ffn_b1, L0_ffn_W2, L0_ffn_b2, L0_ffn_W3, L0_ffn_b3):
    x = np.asarray(L0_attn_x, dtype=np.float32)
    B, L, D = x.shape
    n_cores = 8
    seq_per_core = (B * L) // n_cores
    cores_per_seq = L // seq_per_core

    nc = build_nc(D=D, HF=L0_ffn_W1.shape[0], TC=seq_per_core,
                  TT=min(512, seq_per_core))

    common = {
        "w1": _blockT(np.asarray(L0_ffn_W1, np.float32)),
        "w2": _blockT(np.asarray(L0_ffn_W2, np.float32)),
        "w3": _blockT(np.asarray(L0_ffn_W3, np.float32)),
        "wq": _blockT(np.asarray(L0_attn_Wq, np.float32)),
        "wv": _blockT(np.asarray(L0_attn_Wv, np.float32)),
        "wg": _blockT(np.asarray(L0_attn_Wg, np.float32)),
        "wo": _blockT(np.asarray(L0_attn_Wo, np.float32)),
        "b1c": _colmajor(np.asarray(L0_ffn_b1, np.float32)),
        "b2c": _colmajor(np.asarray(L0_ffn_b2, np.float32)),
        "b3c": _colmajor(np.asarray(L0_ffn_b3, np.float32)),
        "gammac": _colmajor(np.asarray(L0_attn_ln_gamma, np.float32)),
        "betac": _colmajor(np.asarray(L0_attn_ln_beta, np.float32)),
        "hbeta": np.ascontiguousarray(np.asarray(L0_attn_beta, np.float32)[None, :]),
    }
    in_maps = []
    for c in range(n_cores):
        b = c // cores_per_seq
        s = (c % cores_per_seq) * seq_per_core
        m = dict(common)
        m["xin"] = np.ascontiguousarray(x[b, s:s + seq_per_core, :].T)
        er = np.full((1, 8), 1e9, np.float32)
        for i in range(b * cores_per_seq, c):
            er[0, i] = seq_per_core * (c - 1 - i)
        m["erow"] = er
        in_maps.append(m)

    res = run_bass_kernel_spmd(nc, in_maps, core_ids=list(range(n_cores)))
    outp = np.empty((B, L, D), dtype=np.float32)
    for c in range(n_cores):
        b = c // cores_per_seq
        s = (c % cores_per_seq) * seq_per_core
        outp[b, s:s + seq_per_core, :] = res.results[c]["out"].T
    return outp


# version 17
# speedup vs baseline: 1.0427x; 1.0427x over previous
import numpy as np
from contextlib import ExitStack

import concourse.bass as bass
import concourse.mybir as mybir
from concourse.bass_utils import run_bass_kernel_spmd

F32 = mybir.dt.float32
BF16 = mybir.dt.bfloat16
AF = mybir.ActivationFunctionType
ALU = mybir.AluOpType

LN_EPS = 1e-5
LN2 = 0.6931471805599453
TINY = 1.1754943508222875e-38

RACE_SAFE = False
import os
SKIP_SMALL_MM = os.environ.get('SKIP_SMALL_MM', '0') == '1'
BF16_STATS_TEST = os.environ.get('BF16_STATS_TEST', '0') == '1'


class _Op:
    __slots__ = ("stream", "fn", "inc_amt", "needs_inc", "seq", "val", "waits",
                 "tag", "gidx")

    def __init__(self, stream, fn, inc_amt, needs_inc, seq, tag):
        self.stream = stream
        self.fn = fn
        self.inc_amt = inc_amt
        self.needs_inc = needs_inc
        self.seq = seq
        self.val = None
        self.waits = []
        self.tag = tag


class Res:

    __slots__ = ("w", "readers", "name")

    def __init__(self, name=""):
        self.w = None
        self.readers = {}
        self.name = name


class Sched:
    NQS = 8
    NQA = 4
    NQG = 2
    STREAMS = (("pe", "dve", "act", "sync", "gp")
               + tuple(f"qsync{i}" for i in range(NQS))
               + tuple(f"qact{i}" for i in range(NQA))
               + tuple(f"qgp{i}" for i in range(NQG)))
    ISSUER = {}
    for i in range(NQS):
        ISSUER[f"qsync{i}"] = "sync"
    for i in range(NQA):
        ISSUER[f"qact{i}"] = "act"
    for i in range(NQG):
        ISSUER[f"qgp{i}"] = "gp"

    def __init__(self):
        self.ops = {s: [] for s in self.STREAMS}
        self.marks = {s: {} for s in self.STREAMS}
        self.qrr = {"qsync": 0, "qact": 0, "qgp": 0}

    race_safe = False
    skip_same = ("pe",)
    skip_dma_issue = ()

    def _dep(self, op, prod):
        if prod is None:
            return
        ps = prod.stream
        cs = op.stream
        if not self.race_safe:
            p_dma = ps in self.ISSUER
            c_dma = cs in self.ISSUER
            if not p_dma:
                if not c_dma and ps == cs and ps in self.skip_same:
                    return
                if c_dma and self.ISSUER[cs] == ps and ps in self.skip_dma_issue:
                    return
        m = self.marks[cs].get(ps, -1)
        if prod.seq <= m:
            return
        prod.needs_inc = True
        op.waits.append(prod)
        self.marks[cs][ps] = prod.seq

    def add(self, stream, fn, reads=(), writes=(), tag=""):
        dma = stream in ("qsync", "qact", "qgp")
        prev_slot_op = None
        if dma:
            n = {"qsync": self.NQS, "qact": self.NQA, "qgp": self.NQG}[stream]
            slot = self.qrr[stream] % n
            self.qrr[stream] += 1
            stream = f"{stream}{slot}"
            if self.ops[stream]:
                prev_slot_op = self.ops[stream][-1]
        op = _Op(stream, fn, 16 if dma else 1, dma, len(self.ops[stream]), tag)
        if prev_slot_op is not None:
            self._dep(op, prev_slot_op)
        for r in reads:
            self._dep(op, r.w)
        for r in writes:
            self._dep(op, r.w)
            for rd in r.readers.values():
                self._dep(op, rd)
        self.ops[stream].append(op)
        for r in reads:
            prev = r.readers.get(stream)
            if prev is None or prev.seq < op.seq:
                r.readers[stream] = op
        for r in writes:
            r.w = op
            r.readers = {}
        return op

    def finalize(self):
        for s in self.STREAMS:
            c = 0
            for op in self.ops[s]:
                if op.needs_inc:
                    c += op.inc_amt
                    op.val = c

    def replay(self, stream, eng, sems):
        issued = self.ops[stream]
        for op in issued:
            for prod in op.waits:
                assert prod.val is not None, (stream, op.tag, prod.stream, prod.tag)
                eng.wait_ge(sems[prod.stream], prod.val)
            instr = op.fn(eng)
            if op.needs_inc:
                instr.then_inc(sems[op.stream], op.inc_amt)

    def replay_engine(self, eng_name, eng, sems):
        streams = [eng_name]
        for q, issuer in self.ISSUER.items():
            if issuer == eng_name:
                streams.append(q)
        if len(streams) == 1:
            self.replay(eng_name, eng, sems)
            return
        merged = []
        for s in streams:
            merged.extend((op.gidx, s, op) for op in self.ops[s])
        merged.sort(key=lambda t: t[0])
        for _, s, op in merged:
            for prod in op.waits:
                assert prod.val is not None
                eng.wait_ge(sems[prod.stream], prod.val)
            instr = op.fn(eng)
            if op.needs_inc:
                instr.then_inc(sems[op.stream], op.inc_amt)


_GIDX = [0]


def _stamp(op):
    op.gidx = _GIDX[0]
    _GIDX[0] += 1
    return op


def build_nc(D=2048, HF=4096, TC=2048, TT=512):
    _GIDX[0] = 0
    DK = D // 128
    HK = HF // 128
    NH = DK
    NTILES = TC // TT
    H2SEG = HK // 2
    inv_D = 1.0 / D
    assert H2SEG * 2 == HK

    nc = bass.Bass("TRN2", target_bir_lowering=False,
                   detect_race_conditions=RACE_SAFE)

    xin = nc.dram_tensor("xin", [D, TC], F32, kind="ExternalInput").ap()
    out = nc.dram_tensor("out", [D, TC], F32, kind="ExternalOutput").ap()

    wsrc = {}
    wdims = {"w1": (DK, HK), "w2": (HK, HK), "w3": (HK, DK),
             "wq": (DK, DK), "wv": (DK, DK), "wg": (DK, DK), "wo": (DK, DK)}
    for name, (nk, no) in wdims.items():
        wsrc[name] = nc.dram_tensor(name, [no, 128, nk * 128], F32,
                                    kind="ExternalInput").ap()
    whi, wlo = {}, {}
    for name, (nk, no) in wdims.items():
        whi[name] = nc.dram_tensor(name + "_hi", [no, 128, nk * 128], BF16,
                                   kind="Internal").ap()
        wlo[name] = nc.dram_tensor(name + "_lo", [no, 128, nk * 128], BF16,
                                   kind="Internal").ap()

    x1sp = nc.dram_tensor("x1sp", [D, TC], F32, kind="Internal").ap()
    retsp = nc.dram_tensor("retsp", [D, TC], F32, kind="Internal").ap()
    cc_in = nc.dram_tensor("cc_in", [128, NH], F32, kind="Internal").ap()
    cc_out = nc.dram_tensor("cc_out", [8 * 128, NH], F32, kind="Internal").ap()
    erowin = nc.dram_tensor("erow", [1, 8], F32, kind="ExternalInput").ap()

    b1in = nc.dram_tensor("b1c", [128, HK], F32, kind="ExternalInput").ap()
    b2in = nc.dram_tensor("b2c", [128, HK], F32, kind="ExternalInput").ap()
    b3in = nc.dram_tensor("b3c", [128, DK], F32, kind="ExternalInput").ap()
    gammain = nc.dram_tensor("gammac", [128, DK], F32, kind="ExternalInput").ap()
    betain = nc.dram_tensor("betac", [128, DK], F32, kind="ExternalInput").ap()
    hbetain = nc.dram_tensor("hbeta", [1, NH], F32, kind="ExternalInput").ap()

    es = ExitStack()
    sbuf = lambda n, s, d: es.enter_context(nc.sbuf_tensor(n, s, d)).ap()
    psum = lambda n, s: es.enter_context(nc.psum_tensor(n, s, F32)).ap()

    NROT = 6
    UA = DK * TT
    UB = HK * TT
    UC = NROT * TT
    U = sbuf("U", [128, UA + UB + UC], F32)

    def uview(off_f32, n_f32, dtype=F32):
        v = U[:, off_f32:off_f32 + n_f32]
        if dtype == BF16:
            v = v.bitcast(BF16)
        return v

    xh_v = uview(0, DK * TT // 2, BF16)
    xl_v = uview(DK * TT // 2, DK * TT // 2, BF16)
    h2h_v = uview(0, H2SEG * TT // 2, BF16)
    h2l_v = uview(H2SEG * TT // 2, H2SEG * TT // 2, BF16)
    h1h_v = uview(UA, HK * TT // 2, BF16)
    h1l_v = uview(UA + HK * TT // 2, HK * TT // 2, BF16)
    o1_v = uview(UA, DK * TT, F32)
    zh_v = uview(UA + DK * TT, DK * TT // 2, BF16)
    zl_v = uview(UA + DK * TT + DK * TT // 2, DK * TT // 2, BF16)
    SPLIT_PIECE = min(16, UB // (4 * 128)) * 128
    sin_v = [uview(UA + i * SPLIT_PIECE, SPLIT_PIECE, F32) for i in range(2)]
    shi_v = [uview(UA + 2 * SPLIT_PIECE + i * SPLIT_PIECE // 2, SPLIT_PIECE // 2, BF16)
             for i in range(2)]
    slo_v = [uview(UA + 3 * SPLIT_PIECE + i * SPLIT_PIECE // 2, SPLIT_PIECE // 2, BF16)
             for i in range(2)]
    rot_v = [uview(UA + UB + i * TT, TT, F32) for i in range(NROT)]

    scb = sbuf("scb", [128, TT], F32)
    scb2 = sbuf("scb2", [128, TT], F32)
    zeros_t = sbuf("zeros_t", [128, TT], F32)
    S0 = sbuf("S0", [128, DK * TT], F32)
    S1 = sbuf("S1", [128, DK * TT], F32)
    wbuf_h = [sbuf(f"wh{i}", [128, 16 * 128], BF16) for i in range(2)]
    wbuf_l = [sbuf(f"wl{i}", [128, 16 * 128], BF16) for i in range(2)]
    ones_col = sbuf("ones_col", [128, 1], F32)
    ones_row = sbuf("ones_row", [1, 128], F32)
    ones_mat = sbuf("ones_mat", [128, 128], F32)
    ones_mat_b = sbuf("ones_mat_b", [128, 128], BF16)
    mu_sb = sbuf("mu_sb", [128, 512 if 0 else 1] , F32) if False else sbuf("mu_sbx", [128, TT], F32)
    rb_sb = sbuf("rb_sbx", [128, TT], F32)
    lam_cols = sbuf("lam_cols", [128, NH], F32)
    state_cols = sbuf("state_cols", [128, NH], F32)
    b1c = sbuf("b1cs", [128, HK], F32)
    b2c = sbuf("b2cs", [128, HK], F32)
    b3c = sbuf("b3cs", [128, DK], F32)
    gac = sbuf("gacs", [128, DK], F32)
    bec = sbuf("becs", [128, DK], F32)
    hbeta = sbuf("hbetas", [1, NH], F32)
    r_lam = sbuf("r_lam", [1, NH], F32)
    epsc = sbuf("epsc", [128, 1], F32)
    erow = sbuf("erows", [1, 8], F32)
    r_lnlam = sbuf("r_lnlam", [1, NH], F32)
    r_pw = sbuf("r_pw", [1, NH], F32)
    g_sb = sbuf("g_sb", [128, 8 * NH], F32)
    st_tmp = sbuf("st_tmp", [128, NH], F32)

    pmm = [psum(f"pmm{i}", [128, TT]) for i in range(6)]
    pbc0 = psum("pbc0", [128, TT])
    pbc1 = psum("pbc1", [128, TT])

    sch = Sched()
    sch.race_safe = RACE_SAFE
    R = lambda n: Res(n)
    r_xin = R("xin")
    r_out = R("out")
    r_wsrc = {k: R("wsrc_" + k) for k in wdims}
    r_whi = {k: R("whi_" + k) for k in wdims}
    r_wlo = {k: R("wlo_" + k) for k in wdims}
    GRAN = TT // 2
    gA = [R(f"gA{i}") for i in range(UA // GRAN)]
    gB = [R(f"gB{i}") for i in range(UB // GRAN)]
    gS0 = [R(f"gS0_{i}") for i in range(DK)]
    gS1 = [R(f"gS1_{i}") for i in range(DK)]

    def gb_range(off_f32, n_f32):
        return gB[off_f32 // GRAN: -(-(off_f32 + n_f32) // GRAN)]

    r_rot = [R(f"rot{i}") for i in range(NROT)]
    r_wbh = [R(f"wbh{i}") for i in range(2)]
    r_wbl = [R(f"wbl{i}") for i in range(2)]
    r_consts = R("consts")
    r_state = R("state_cols")
    r_rows = {id(x): R("row") for x in ()}
    r_rmu, r_rm2, r_rt, r_rvar, r_rsd, r_rrr = (R("rmu"), R("rm2"), R("rt"),
                                                R("rvar"), R("rsd"), R("rrr"))
    r_pmm = [R(f"pmm{i}") for i in range(6)]
    r_pbc0, r_pbc1 = R("pbc0"), R("pbc1")
    r_lnl, r_erow, r_pw2 = R("lnlam"), R("erow"), R("pw")
    r_mub, r_rbb = R("mu_b"), R("r_b")
    r_scb = R("scb")
    r_scb2 = R("scb2")
    r_retsp = R("retsp")
    r_x1sp, r_ccin, r_ccout, r_gsb, r_sttmp = (R("x1sp"), R("ccin"), R("ccout"),
                                               R("gsb"), R("sttmp"))

    def op(stream, fn, reads=(), writes=(), tag=""):
        return _stamp(sch.add(stream, fn, reads, writes, tag))

    op("dve", lambda e: e.memset(ones_col[:, :], 1.0), writes=[r_consts], tag="ones_col")
    op("dve", lambda e: e.memset(ones_row[:, :], 1.0), reads=[r_consts], writes=[r_consts], tag="ones_row")
    op("dve", lambda e: e.memset(ones_mat[:, :], 1.0), reads=[r_consts], writes=[r_consts], tag="ones_mat")
    op("dve", lambda e: e.memset(ones_mat_b[:, :], 1.0), reads=[r_consts], writes=[r_consts], tag="ones_mat_b")
    op("dve", lambda e: e.memset(state_cols[:, :], 0.0), writes=[r_state], tag="state0")
    op("dve", lambda e: e.memset(zeros_t[:, :], 0.0), reads=[r_consts], writes=[r_consts], tag="zeros")
    op("dve", lambda e: e.memset(epsc[:, :], LN_EPS), reads=[r_consts],
       writes=[r_consts], tag="eps")
    for dst, src in ((b1c, b1in), (b2c, b2in), (gac, gammain), (bec, betain),
                     (hbeta, hbetain)):
        op("qsync", lambda e, d=dst, s=src: e.dma_start(out=d[:, :], in_=s[:, :]),
           writes=[r_consts], tag="ld_const")
    op("qsync", lambda e: e.dma_start(out=b3c[:, :], in_=b3in[:, :]),
       writes=[r_consts], tag="ld_b3")
    op("dve", lambda e: e.tensor_scalar(b3c[:, :], b3c[:, :], 0.5, None, ALU.mult),
       reads=[r_consts], writes=[r_consts], tag="b3half")
    op("act", lambda e: e.activation(r_lam[:, :], hbeta[:, :], AF.Exp, scale=-LN2),
       reads=[r_consts], writes=[r_rmu], tag="lam_exp")
    op("dve", lambda e: e.tensor_scalar(r_lam[:, :], r_lam[:, :], -1.0, 1.0,
                                        ALU.mult, ALU.add),
       reads=[r_rmu], writes=[r_rmu], tag="lam_1m")
    op("dve", lambda e: e.tensor_scalar(r_lam[:, :], r_lam[:, :], TINY, 1.0 - 1e-9,
                                        ALU.max, ALU.min),
       reads=[r_rmu], writes=[r_rmu], tag="lam_clip")
    op("pe", lambda e: e.matmul(pbc0[:, 0:NH], ones_row[:, :], r_lam[:, :],
                                start=True, stop=True),
       reads=[r_rmu, r_consts], writes=[r_pbc0], tag="lam_bcast")
    op("act", lambda e: e.copy(lam_cols[:, :], pbc0[:, 0:NH]),
       reads=[r_pbc0], writes=[r_consts], tag="lam_cols")
    op("act", lambda e: e.activation(r_lnlam[:, :], r_lam[:, :], AF.Ln),
       reads=[r_rmu], writes=[r_lnl], tag="lnlam")
    op("qsync", lambda e: e.dma_start(out=erow[:, :], in_=erowin[:, :]),
       writes=[r_erow], tag="ld_erow")

    scr_piece = {}
    r_wsc = {}

    def _mk_scr(name, piece):
        nk, no = wdims[name]
        scr_piece[name] = piece
        npieces = (nk * 128 + piece - 1) // piece
        r_wsc[name] = {pol: [[R(f"sc_{name}_{pol}_{oc}_{p}") for p in range(npieces)]
                             for oc in range(no)] for pol in ("hi", "lo")}

    def split_gen(name, bulk):
        nk, no = wdims[name]
        piece = scr_piece[name]
        cnt = [0]
        for oc in range(no):
            line = nk * 128
            for pidx, k0 in enumerate(range(0, line, piece)):
                n = min(piece, line - k0)
                if bulk:
                    rb = cnt[0] % 2
                    vin = sin_v[rb]
                    rin = gb_range(rb * SPLIT_PIECE, SPLIT_PIECE)
                    vhi, vlo = shi_v[rb], slo_v[rb]
                    rhl = gb_range(2 * SPLIT_PIECE + rb * SPLIT_PIECE // 2,
                                   SPLIT_PIECE // 2)
                    rhl2 = gb_range(3 * SPLIT_PIECE + rb * SPLIT_PIECE // 2,
                                    SPLIT_PIECE // 2)
                else:
                    vin, rin0 = rot()
                    slot, rhl0 = rot()
                    vhi = slot[:, 0:piece // 2].bitcast(BF16)
                    vlo = slot[:, piece // 2:piece].bitcast(BF16)
                    rin, rhl, rhl2 = [rin0], [rhl0], [rhl0]
                cnt[0] += 1
                op("qsync",
                   lambda e, oc=oc, k0=k0, n=n, vin=vin:
                       e.dma_start(out=vin[:, 0:n], in_=wsrc[name][oc, :, k0:k0 + n]),
                   reads=[r_wsrc[name]], writes=rin, tag=f"spl_ld_{name}")
                op("dve",
                   lambda e, n=n, vin=vin, vhi=vhi:
                       e.tensor_copy(vhi[:, 0:n], vin[:, 0:n]),
                   reads=rin, writes=rhl, tag="spl_hi")
                op("dve",
                   lambda e, n=n, vin=vin, vhi=vhi, vlo=vlo:
                       e.tensor_tensor(vlo[:, 0:n], vin[:, 0:n], vhi[:, 0:n],
                                       ALU.subtract),
                   reads=rin + rhl, writes=rhl2, tag="spl_lo")
                op("qact",
                   lambda e, oc=oc, k0=k0, n=n, vhi=vhi:
                       e.dma_start(out=whi[name][oc, :, k0:k0 + n], in_=vhi[:, 0:n]),
                   reads=rhl, writes=[r_wsc[name]["hi"][oc][pidx]],
                   tag=f"spl_st_hi_{name}")
                op("qact",
                   lambda e, oc=oc, k0=k0, n=n, vlo=vlo:
                       e.dma_start(out=wlo[name][oc, :, k0:k0 + n], in_=vlo[:, 0:n]),
                   reads=rhl2, writes=[r_wsc[name]["lo"][oc][pidx]],
                   tag=f"spl_st_lo_{name}")
                produced[name] += 1
                yield

    produced = {name: 0 for name in wdims}
    pump_q = []

    def pump(n):
        while n > 0 and pump_q:
            try:
                next(pump_q[0])
                n -= 1
            except StopIteration:
                pump_q.pop(0)

    def pump_for(name, npieces):
        while produced[name] < npieces and pump_q:
            try:
                next(pump_q[0])
            except StopIteration:
                pump_q.pop(0)

    BULK = ("w1", "w2")
    PUMPED = ("w3", "wv", "wq", "wg", "wo")
    for name in BULK:
        _mk_scr(name, SPLIT_PIECE)
    for name in PUMPED:
        _mk_scr(name, TT)
    for name in BULK:
        for _ in split_gen(name, bulk=True):
            pass
    for name in PUMPED:
        pump_q.append(split_gen(name, bulk=False))

    x3 = xin.rearrange("(k p) t -> p k t", p=128)
    out3 = out.rearrange("(k p) t -> p k t", p=128)

    def load_xtile(ti):
        t0 = ti * TT
        src = x3[:, :, t0:t0 + TT]
        dst = S0.rearrange("p (k t) -> p k t", t=TT)
        op("qsync", lambda e, src=src, dst=dst: e.dma_start(out=dst, in_=src),
           reads=[r_xin], writes=gS0, tag=f"x_ld{ti}")

    def store_tile(ti):
        t0 = ti * TT
        dst = out3[:, :, t0:t0 + TT]
        src = S1.rearrange("p (k t) -> p k t", t=TT)
        op("qact", lambda e, src=src, dst=dst: e.dma_start(out=dst, in_=src),
           reads=gS1, writes=[r_out], tag=f"out_st{ti}")

    rotc = [0]

    def rot():
        i = rotc[0] % NROT
        rotc[0] += 1
        return rot_v[i], r_rot[i]

    sc_state = {"n": 0}

    def stats_pair_ops(sl_ap, rS, first, last, tag, sqv, sq_res):
        op("act", lambda e, sl=sl_ap, rq=sqv: e.activation(rq[:, :], sl, AF.Square),
           reads=rS, writes=sq_res, tag=tag + "_sq")
        op("pe", lambda e, sl=sl_ap, first=first, last=last: e.matmul(
            pbc0[:, :], ones_mat[:, :], sl, start=first, stop=last),
           reads=rS + [r_consts], writes=[r_pbc0], tag=tag + "_s1")
        op("pe", lambda e, rq=sqv, first=first, last=last: e.matmul(
            pbc1[:, :], ones_mat[:, :], rq[:, :], start=first, stop=last),
           reads=sq_res + [r_consts], writes=[r_pbc1], tag=tag + "_s2")

    def stats_norm(S, rS, res_h, res_l, xh, xl, tag, stats_done=False):
        if not stats_done:
            for kc in range(DK):
                sl = S[:, kc * TT:(kc + 1) * TT]
                qv, qr = sqA(kc)
                stats_pair_ops(sl, [rS[kc]], kc == 0, kc == DK - 1,
                               tag + f"_st{kc}", qv, qr)
        finish_stats(tag)
        for kc in range(DK):
            sl = S[:, kc * TT:(kc + 1) * TT]
            rv0, rr0 = rot()
            rv1, rr1 = rot()
            op("dve", lambda e, sl=sl, rv0=rv0: e.tensor_tensor(
                rv0[:, :], sl, mu_sb[:, :], ALU.subtract),
               reads=[rS[kc], r_mub], writes=[rr0], tag=tag + "_sub")
            op("dve", lambda e, rv0=rv0, rv1=rv1: e.tensor_tensor(
                rv1[:, :], rv0[:, :], rb_sb[:, :], ALU.mult),
               reads=[rr0, r_rbb], writes=[rr1], tag=tag + "_mul")
            hsl = xh[:, kc * TT:(kc + 1) * TT]
            lsl = xl[:, kc * TT:(kc + 1) * TT]
            op("act", lambda e, rv1=rv1, hsl=hsl: e.copy(hsl, rv1[:, :]),
               reads=[rr1], writes=[res_h[kc]], tag=tag + "_xh")
            op("dve", lambda e, rv1=rv1, hsl=hsl, lsl=lsl: e.tensor_tensor(
                lsl, rv1[:, :], hsl, ALU.subtract),
               reads=[rr1, res_h[kc]], writes=[res_l[kc]], tag=tag + "_xl")

    def finish_stats(tag):
        rv0, rr0 = rot()
        rv1, rr1 = rot()
        op("dve", lambda e: e.tensor_scalar(mu_sb[:, :], pbc0[:, :], inv_D, None,
                                            ALU.mult),
           reads=[r_pbc0], writes=[r_mub], tag=tag + "_mu")
        op("dve", lambda e, rv0=rv0: e.tensor_scalar(rv0[:, :], pbc1[:, :], inv_D,
                                                     None, ALU.mult),
           reads=[r_pbc1], writes=[rr0], tag=tag + "_m2")
        op("dve", lambda e, rv1=rv1: e.tensor_tensor(
            rv1[:, :], mu_sb[:, :], mu_sb[:, :], ALU.mult),
           reads=[r_mub], writes=[rr1], tag=tag + "_musq")
        op("dve", lambda e, rv0=rv0, rv1=rv1: e.tensor_tensor(
            rv0[:, :], rv0[:, :], rv1[:, :], ALU.subtract),
           reads=[rr0, rr1], writes=[rr0], tag=tag + "_var")
        op("act", lambda e, rv0=rv0: e.activation(rv0[:, :], rv0[:, :], AF.Sqrt,
                                                  bias=epsc[:, :]),
           reads=[rr0, r_consts], writes=[rr0], tag=tag + "_sd")
        op("dve", lambda e, rv0=rv0: e.reciprocal(rb_sb[:, :], rv0[:, :]),
           reads=[rr0], writes=[r_rbb], tag=tag + "_rr")

    wrot = [0]

    def mm_group(wname, oc, nkc, kc_base, mvh, mvl, res_h, res_l, ps_i, tag):
        nk, no = wdims[wname]
        ps = pmm[ps_i]
        rps = r_pmm[ps_i]
        pieces = [(p0, min(nkc, p0 + 16)) for p0 in range(0, nkc, 16)]
        first = True
        if wname in PUMPED:
            pe_sz = scr_piece[wname]
            need = -(-((kc_base + nkc) * 128) // pe_sz)
            pump_for(wname, oc * ((wdims[wname][0] * 128 + pe_sz - 1) // pe_sz) + need)
        pump(3)
        for (p0, p1) in pieces:
            wb = wrot[0] % 2
            wrot[0] += 1
            npc = (p1 - p0) * 128
            k0 = (kc_base + p0) * 128
            pe_sz = scr_piece[wname]
            i0, i1 = k0 // pe_sz, -(-(k0 + npc) // pe_sz)
            sc_hi = r_wsc[wname]["hi"][oc][i0:i1]
            sc_lo = r_wsc[wname]["lo"][oc][i0:i1]
            for rr_ in sc_hi + sc_lo:
                assert rr_.w is not None, (wname, oc, tag, "scratch not yet split")
            op("qsync", lambda e, oc=oc, k0=k0, npc=npc, wb=wb: e.dma_start(
                out=wbuf_h[wb][:, 0:npc], in_=whi[wname][oc, :, k0:k0 + npc]),
               reads=sc_hi, writes=[r_wbh[wb]], tag=tag + "_lh")
            op("qsync", lambda e, oc=oc, k0=k0, npc=npc, wb=wb: e.dma_start(
                out=wbuf_l[wb][:, 0:npc], in_=wlo[wname][oc, :, k0:k0 + npc]),
               reads=sc_lo, writes=[r_wbl[wb]], tag=tag + "_ll")
            last_piece = (p1 == nkc)
            for kc in range(p0, p1):
                koff = (kc - p0) * 128
                mh = mvh[:, kc * TT:(kc + 1) * TT]
                ml = mvl[:, kc * TT:(kc + 1) * TT]
                last = last_piece and (kc == p1 - 1)
                op("pe", lambda e, ps=ps, wb=wb, koff=koff, mh=mh, first=first:
                   e.matmul(ps[:, :], wbuf_h[wb][:, koff:koff + 128], mh,
                            start=first, stop=False),
                   reads=[r_wbh[wb], res_h(kc)], writes=[rps], tag=tag + "_hh")
                first = False
                op("pe", lambda e, ps=ps, wb=wb, koff=koff, ml=ml:
                   e.matmul(ps[:, :], wbuf_h[wb][:, koff:koff + 128], ml,
                            start=False, stop=False),
                   reads=[r_wbh[wb], res_l(kc)], writes=[rps], tag=tag + "_hl")
                op("pe", lambda e, ps=ps, wb=wb, koff=koff, mh=mh, last=last:
                   e.matmul(ps[:, :], wbuf_l[wb][:, koff:koff + 128], mh,
                            start=False, stop=last),
                   reads=[r_wbl[wb], res_h(kc)], writes=[rps], tag=tag + "_lh")
        return ps, rps

    WLAG = 4

    xg_h = lambda kc: gA[kc]
    xg_l = lambda kc: gA[DK + kc]
    h1g_h = lambda kc: gB[kc]
    h1g_l = lambda kc: gB[HK + kc]
    h2g_h = lambda kc: gA[kc]
    h2g_l = lambda kc: gA[H2SEG + kc]
    zg_h = lambda kc: gB[2 * DK + kc]
    zg_l = lambda kc: gB[3 * DK + kc]
    o1g = lambda hc: [gB[2 * hc], gB[2 * hc + 1]]

    sqA = lambda kc: (uview(kc * TT, TT), [gA[2 * kc], gA[2 * kc + 1]])
    sqB = lambda c: (uview(UA + c * TT, TT), [gB[2 * c], gB[2 * c + 1]])
    sqZ = lambda c: (uview(UA + DK * TT + c * TT, TT),
                     [gB[2 * DK + 2 * c], gB[2 * DK + 2 * c + 1]])

    def ffn(rSin, Sin, rSout, Sout, tag, stats_done=False, weave_out=False):
        pend = []
        stats_norm(Sin, rSin, [xg_h(k) for k in range(DK)],
                   [xg_l(k) for k in range(DK)], xh_v, xl_v, tag + "_ln",
                   stats_done=stats_done)
        for oc in range(HK):
            ps, rps = mm_group("w1", oc, DK, 0, xh_v, xl_v, xg_h, xg_l, oc % 6, tag + f"_w1_{oc}")
            rv, rr = rot()
            op("act", lambda e, ps=ps, rv=rv, oc=oc: e.activation(
                rv[:, :], ps[:, :], AF.Relu, bias=b1c[:, oc:oc + 1]),
               reads=[rps, r_consts], writes=[rr], tag=tag + "_h1ev")
            hsl = h1h_v[:, oc * TT:(oc + 1) * TT]
            lsl = h1l_v[:, oc * TT:(oc + 1) * TT]
            op("act", lambda e, rv=rv, hsl=hsl: e.copy(hsl, rv[:, :]),
               reads=[rr], writes=[h1g_h(oc)], tag=tag + "_h1h")
            op("dve", lambda e, rv=rv, hsl=hsl, lsl=lsl: e.tensor_tensor(
                lsl, rv[:, :], hsl, ALU.subtract),
               reads=[rr, h1g_h(oc)], writes=[h1g_l(oc)], tag=tag + "_h1l")
        for seg in range(2):
            for i in range(H2SEG):
                oc2 = seg * H2SEG + i
                ps, rps = mm_group("w2", oc2, HK, 0, h1h_v, h1l_v, h1g_h, h1g_l,
                                   oc2 % 6, tag + f"_w2_{oc2}")
                rv, rr = rot()
                op("act", lambda e, ps=ps, rv=rv, oc2=oc2: e.activation(
                    rv[:, :], ps[:, :], AF.Relu, bias=b2c[:, oc2:oc2 + 1]),
                   reads=[rps, r_consts], writes=[rr], tag=tag + "_h2ev")
                hsl = h2h_v[:, i * TT:(i + 1) * TT]
                lsl = h2l_v[:, i * TT:(i + 1) * TT]
                op("act", lambda e, rv=rv, hsl=hsl: e.copy(hsl, rv[:, :]),
                   reads=[rr], writes=[h2g_h(i)], tag=tag + "_h2h")
                op("dve", lambda e, rv=rv, hsl=hsl, lsl=lsl: e.tensor_tensor(
                    lsl, rv[:, :], hsl, ALU.subtract),
                   reads=[rr, h2g_h(i)], writes=[h2g_l(i)], tag=tag + "_h2l")
            for oc3 in range(DK):
                ps, rps = mm_group("w3", oc3, H2SEG, seg * H2SEG, h2h_v, h2l_v,
                                   h2g_h, h2g_l, oc3 % 6, tag + f"_w3_{seg}_{oc3}")
                ssl_in = Sin[:, oc3 * TT:(oc3 + 1) * TT]
                ssl_out = Sout[:, oc3 * TT:(oc3 + 1) * TT]
                if seg == 0:
                    rv, rr = rot()
                    op("act", lambda e, ps=ps, rv=rv, oc3=oc3: e.activation(
                        rv[:, :], ps[:, :], AF.Identity, bias=b3c[:, oc3:oc3 + 1],
                        scale=0.5),
                       reads=[rps, r_consts], writes=[rr], tag=tag + "_w3ev")
                    op("dve", lambda e, rv=rv, a=ssl_in, o=ssl_out: e.tensor_tensor(
                        o, rv[:, :], a, ALU.add),
                       reads=[rr, rSin[oc3]], writes=[rSout[oc3]], tag=tag + "_res")
                else:
                    op("dve", lambda e, ps=ps, o=ssl_out: e.scalar_tensor_tensor(
                        o, ps[:, :], 0.5, o, ALU.mult, ALU.add),
                       reads=[rps, rSout[oc3]], writes=[rSout[oc3]], tag=tag + "_res2")
                    if weave_out:
                        pend.append((ssl_out, oc3))
                        if len(pend) > WLAG:
                            a_, i_ = pend.pop(0)
                            qv, qr = sqB(i_)
                            stats_pair_ops(a_, [rSout[i_]], i_ == 0, i_ == DK - 1,
                                           tag + f"_wst{i_}", qv, qr)

        for a_, i_ in pend:
            qv, qr = sqB(i_)
            stats_pair_ops(a_, [rSout[i_]], i_ == 0, i_ == DK - 1, tag + f"_wst{i_}",
                           qv, qr)

    def spill_x1(ti):
        t0 = ti * TT
        x1sp3 = x1sp.rearrange("(k p) t -> p k t", p=128)
        dst = x1sp3[:, :, t0:t0 + TT]
        src = S1.rearrange("p (k t) -> p k t", t=TT)
        op("qact", lambda e, src=src, dst=dst: e.dma_start(out=dst, in_=src),
           reads=gS1, writes=[r_x1sp], tag=f"x1_sp{ti}")

    def load_x1(ti):
        t0 = ti * TT
        x1sp3 = x1sp.rearrange("(k p) t -> p k t", p=128)
        src = x1sp3[:, :, t0:t0 + TT]
        dst = S1.rearrange("p (k t) -> p k t", t=TT)
        op("qsync", lambda e, src=src, dst=dst: e.dma_start(out=dst, in_=src),
           reads=[r_x1sp], writes=gS1, tag=f"x1_ld{ti}")

    ret3 = retsp.rearrange("(h p) t -> p h t", p=128)

    def attn_front(rSin, Sin, ti, tag, stats_done=False):
        stats_norm(Sin, rSin, [xg_h(k) for k in range(DK)],
                   [xg_l(k) for k in range(DK)], xh_v, xl_v, tag + "_ln",
                   stats_done=stats_done)
        for hc in range(NH):
            ps, rps = mm_group("wv", hc, DK, 0, xh_v, xl_v, xg_h, xg_l, hc % 3, tag + f"_v{hc}")
            rv, rrv = rot()
            op("dve", lambda e, ps=ps, rv=rv: e.tensor_copy(rv[:, :], ps[:, :]),
               reads=[rps], writes=[rrv], tag=tag + "_vev")
            rrt, rrrt = (scb, r_scb) if hc % 2 == 0 else (scb2, r_scb2)
            lam_b = lam_cols[:, hc:hc + 1].broadcast_to([128, TT])
            op("dve", lambda e, rv=rv, rrt=rrt, lam_b=lam_b, hc=hc:
               e.tensor_tensor_scan(rrt[:, :], lam_b, rv[:, :],
                                    state_cols[:, hc:hc + 1], ALU.mult, ALU.add),
               reads=[rrv, r_consts, r_state], writes=[rrrt], tag=tag + "_scan")
            op("dve", lambda e, rrt=rrt, hc=hc: e.tensor_copy(
                state_cols[:, hc:hc + 1], rrt[:, TT - 1:TT]),
               reads=[rrrt], writes=[r_state], tag=tag + "_stupd")
            dst = ret3[:, hc, ti * TT:(ti + 1) * TT]
            op("qact", lambda e, rrt=rrt, dst=dst: e.dma_start(out=dst, in_=rrt[:, :]),
               reads=[rrrt], writes=[r_retsp], tag=tag + "_rsp")

    def exchange(tag="xc"):
        op("qgp", lambda e: e.dma_start(out=cc_in[:, :], in_=state_cols[:, :]),
           reads=[r_state], writes=[r_ccin], tag=tag + "_in")
        op("gp", lambda e: e.collective_compute(
            "AllGather", ALU.bypass, replica_groups=[list(range(8))],
            ins=[cc_in[:, :].opt()], outs=[cc_out[:, :].opt()]),
           reads=[r_ccin], writes=[r_ccout], tag=tag + "_ag")
        src = cc_out.rearrange("(i p) h -> p i h", p=128)
        dst = g_sb.rearrange("p (i h) -> p i h", h=NH)
        op("qgp", lambda e, src=src, dst=dst: e.dma_start(out=dst, in_=src),
           reads=[r_ccout], writes=[r_gsb], tag=tag + "_out")
        op("dve", lambda e: e.memset(state_cols[:, :], 0.0),
           reads=[r_state], writes=[r_state], tag=tag + "_zero")
        for i in range(8):
            op("dve", lambda e, i=i: e.tensor_scalar(
                r_pw[:, :], r_lnlam[:, :], erow[0:1, i:i + 1], None, ALU.mult),
               reads=[r_lnl, r_erow], writes=[r_pw2], tag=tag + "_pw")
            op("act", lambda e: e.activation(r_pw[:, :], r_pw[:, :], AF.Exp),
               reads=[r_pw2], writes=[r_pw2], tag=tag + "_pwe")
            op("pe", lambda e: e.matmul(pbc0[:, 0:NH], ones_row[:, :], r_pw[:, :],
                                        start=True, stop=True),
               reads=[r_pw2, r_consts], writes=[r_pbc0], tag=tag + "_pwb")
            op("dve", lambda e, i=i: e.tensor_tensor(
                st_tmp[:, :], g_sb[:, i * NH:(i + 1) * NH], pbc0[:, 0:NH], ALU.mult),
               reads=[r_gsb, r_pbc0], writes=[r_sttmp], tag=tag + "_mul")
            op("dve", lambda e: e.tensor_tensor(
                state_cols[:, :], state_cols[:, :], st_tmp[:, :], ALU.add),
               reads=[r_state, r_sttmp], writes=[r_state], tag=tag + "_acc")

    def attn(rSin, Sin, rSout, Sout, ti, tag, stats_done=False, weave_out=False):
        pend2 = []
        stats_norm(Sin, rSin, [xg_h(k) for k in range(DK)],
                   [xg_l(k) for k in range(DK)], xh_v, xl_v, tag + "_ln",
                   stats_done=stats_done)
        for hc in range(NH):
            ps2, rps2 = mm_group("wq", hc, DK, 0, xh_v, xl_v, xg_h, xg_l, 3 + hc % 3,
                                 tag + f"_q{hc}")
            rv, rrv = rot()
            srcr = ret3[:, hc, ti * TT:(ti + 1) * TT]
            op("qsync", lambda e, rv=rv, srcr=srcr: e.dma_start(out=rv[:, :], in_=srcr),
               reads=[r_retsp], writes=[rrv], tag=tag + "_rld")
            rrt, rrrt = (scb, r_scb) if hc % 2 == 0 else (scb2, r_scb2)
            lam_b = lam_cols[:, hc:hc + 1].broadcast_to([128, TT])
            op("dve", lambda e, rrt=rrt, lam_b=lam_b, hc=hc:
               e.tensor_tensor_scan(rrt[:, :], lam_b, zeros_t[:, :],
                                    state_cols[:, hc:hc + 1], ALU.mult, ALU.add),
               reads=[r_consts, r_state], writes=[rrrt], tag=tag + "_dscan")
            op("dve", lambda e, rrt=rrt, hc=hc: e.tensor_copy(
                state_cols[:, hc:hc + 1], rrt[:, TT - 1:TT]),
               reads=[rrrt], writes=[r_state], tag=tag + "_stupd")
            op("dve", lambda e, rv=rv, rrt=rrt: e.tensor_tensor(
                rv[:, :], rv[:, :], rrt[:, :], ALU.add),
               reads=[rrv, rrrt], writes=[rrv], tag=tag + "_rcor")
            osl = o1_v[:, hc * TT:(hc + 1) * TT]
            op("dve", lambda e, ps2=ps2, rv=rv, osl=osl: e.tensor_tensor(
                osl, ps2[:, :], rv[:, :], ALU.mult),
               reads=[rps2, rrv], writes=o1g(hc), tag=tag + "_o1")
            qv, qr = sqZ(hc)
            stats_pair_ops(osl, o1g(hc), hc == 0, hc == NH - 1, tag + f"_ist{hc}",
                           qv, qr)
        finish_stats(tag + "_i")
        for hc in range(NH):
            ps, rps = mm_group("wg", hc, DK, 0, xh_v, xl_v, xg_h, xg_l, hc % 3, tag + f"_g{hc}")
            rsg, rrsg = rot()
            op("act", lambda e, ps=ps, rsg=rsg: e.activation(rsg[:, :], ps[:, :], AF.Silu),
               reads=[rps], writes=[rrsg], tag=tag + "_silu")
            osl = o1_v[:, hc * TT:(hc + 1) * TT]
            rv0, rr0 = rot()
            op("dve", lambda e, osl=osl, rv0=rv0: e.tensor_tensor(
                rv0[:, :], osl, mu_sb[:, :], ALU.subtract),
               reads=o1g(hc) + [r_mub], writes=[rr0], tag=tag + "_ynsub")
            op("dve", lambda e, rv0=rv0: e.tensor_tensor(
                rv0[:, :], rv0[:, :], rb_sb[:, :], ALU.mult),
               reads=[rr0, r_rbb], writes=[rr0], tag=tag + "_ynmul")
            op("dve", lambda e, rv0=rv0, hc=hc: e.tensor_scalar(
                rv0[:, :], rv0[:, :], gac[:, hc:hc + 1], bec[:, hc:hc + 1],
                ALU.mult, ALU.add),
               reads=[rr0, r_consts], writes=[rr0], tag=tag + "_gb")
            op("dve", lambda e, rv0=rv0, rsg=rsg: e.tensor_tensor(
                rv0[:, :], rv0[:, :], rsg[:, :], ALU.mult),
               reads=[rr0, rrsg], writes=[rr0], tag=tag + "_gate")
            hsl = zh_v[:, hc * TT:(hc + 1) * TT]
            lsl = zl_v[:, hc * TT:(hc + 1) * TT]
            op("act", lambda e, rv0=rv0, hsl=hsl: e.copy(hsl, rv0[:, :]),
               reads=[rr0], writes=[zg_h(hc)], tag=tag + "_zh")
            op("dve", lambda e, rv0=rv0, hsl=hsl, lsl=lsl: e.tensor_tensor(
                lsl, rv0[:, :], hsl, ALU.subtract),
               reads=[rr0, zg_h(hc)], writes=[zg_l(hc)], tag=tag + "_zl")
        for oc in range(DK):
            ps, rps = mm_group("wo", oc, DK, 0, zh_v, zl_v, zg_h, zg_l, oc % 6, tag + f"_wo{oc}")
            a = Sin[:, oc * TT:(oc + 1) * TT]
            o = Sout[:, oc * TT:(oc + 1) * TT]
            op("dve", lambda e, ps=ps, a=a, o=o: e.tensor_tensor(o, ps[:, :], a, ALU.add),
               reads=[rps, rSin[oc]], writes=[rSout[oc]], tag=tag + "_res")
            if weave_out:
                pend2.append((o, oc))
                if len(pend2) > WLAG:
                    a_, i_ = pend2.pop(0)
                    qv, qr = sqB(i_)
                    stats_pair_ops(a_, [rSout[i_]], i_ == 0, i_ == DK - 1,
                                   tag + f"_wst{i_}", qv, qr)
        for a_, i_ in pend2:
            qv, qr = sqB(i_)
            stats_pair_ops(a_, [rSout[i_]], i_ == 0, i_ == DK - 1, tag + f"_wst{i_}",
                           qv, qr)

    for ti in range(NTILES):
        load_xtile(ti)
        ffn(gS0, S0, gS1, S1, f"t{ti}f1", weave_out=True)
        spill_x1(ti)
        attn_front(gS1, S1, ti, f"t{ti}af", stats_done=True)
    exchange()
    for ti in range(NTILES):
        load_x1(ti)
        attn(gS1, S1, gS0, S0, ti, f"t{ti}at", weave_out=True)
        ffn(gS0, S0, gS1, S1, f"t{ti}f2", stats_done=True)
        store_tile(ti)

    sch.finalize()
    sems = {}
    for s in Sched.STREAMS:
        sems[s] = es.enter_context(nc.semaphore("sem_" + s))

    final_waits = []
    for s in Sched.STREAMS:
        if s.startswith("qact") and sch.ops[s]:
            last = sch.ops[s][-1]
            assert last.val is not None
            final_waits.append((s, last.val))

    with nc.Block() as block:
        @block.tensor
        def _(eng):
            sch.replay_engine("pe", eng, sems)

        @block.vector
        def _(eng):
            sch.replay_engine("dve", eng, sems)

        @block.scalar
        def _(eng):
            sch.replay_engine("act", eng, sems)

        @block.gpsimd
        def _(eng):
            sch.replay_engine("gp", eng, sems)

        @block.sync
        def _(eng):
            sch.replay_engine("sync", eng, sems)
            for s, v in final_waits:
                eng.wait_ge(sems[s], v)

    es.close()
    return nc


def _blockT(W):
    WT = np.ascontiguousarray(W.T)
    K, O = WT.shape
    nk, no = K // 128, O // 128
    a = WT.reshape(nk, 128, no, 128).transpose(2, 1, 0, 3)
    return np.ascontiguousarray(a.reshape(no, 128, nk * 128))


def _colmajor(v):
    return np.ascontiguousarray(v.reshape(-1, 128).T)


def kernel(L0_attn_x, L0_attn_Wq, L0_attn_Wv, L0_attn_Wo, L0_attn_Wg,
           L0_attn_ln_gamma, L0_attn_ln_beta, L0_attn_beta,
           L0_ffn_W1, L0_ffn_b1, L0_ffn_W2, L0_ffn_b2, L0_ffn_W3, L0_ffn_b3):
    x = np.asarray(L0_attn_x, dtype=np.float32)
    B, L, D = x.shape
    n_cores = 8
    seq_per_core = (B * L) // n_cores
    cores_per_seq = L // seq_per_core

    nc = build_nc(D=D, HF=L0_ffn_W1.shape[0], TC=seq_per_core,
                  TT=min(512, seq_per_core))

    common = {
        "w1": _blockT(np.asarray(L0_ffn_W1, np.float32)),
        "w2": _blockT(np.asarray(L0_ffn_W2, np.float32)),
        "w3": _blockT(np.asarray(L0_ffn_W3, np.float32)),
        "wq": _blockT(np.asarray(L0_attn_Wq, np.float32)),
        "wv": _blockT(np.asarray(L0_attn_Wv, np.float32)),
        "wg": _blockT(np.asarray(L0_attn_Wg, np.float32)),
        "wo": _blockT(np.asarray(L0_attn_Wo, np.float32)),
        "b1c": _colmajor(np.asarray(L0_ffn_b1, np.float32)),
        "b2c": _colmajor(np.asarray(L0_ffn_b2, np.float32)),
        "b3c": _colmajor(np.asarray(L0_ffn_b3, np.float32)),
        "gammac": _colmajor(np.asarray(L0_attn_ln_gamma, np.float32)),
        "betac": _colmajor(np.asarray(L0_attn_ln_beta, np.float32)),
        "hbeta": np.ascontiguousarray(np.asarray(L0_attn_beta, np.float32)[None, :]),
    }
    in_maps = []
    for c in range(n_cores):
        b = c // cores_per_seq
        s = (c % cores_per_seq) * seq_per_core
        m = dict(common)
        m["xin"] = np.ascontiguousarray(x[b, s:s + seq_per_core, :].T)
        er = np.full((1, 8), 1e9, np.float32)
        for i in range(b * cores_per_seq, c):
            er[0, i] = seq_per_core * (c - 1 - i)
        m["erow"] = er
        in_maps.append(m)

    res = run_bass_kernel_spmd(nc, in_maps, core_ids=list(range(n_cores)))
    outp = np.empty((B, L, D), dtype=np.float32)
    for c in range(n_cores):
        b = c // cores_per_seq
        s = (c % cores_per_seq) * seq_per_core
        outp[b, s:s + seq_per_core, :] = res.results[c]["out"].T
    return outp
